# version 27
# speedup vs baseline: 1.1366x; 1.0788x over previous
import numpy as np
from contextlib import ExitStack

import concourse.bass as bass
import concourse.mybir as mybir
from concourse.bass_utils import run_bass_kernel_spmd

F32 = mybir.dt.float32
BF16 = mybir.dt.bfloat16
AF = mybir.ActivationFunctionType
ALU = mybir.AluOpType
AX = mybir.AxisListType

NEG = -300000.0
MAGIC = 12582912.0
C_ = 1024
E_, F_, FS_ = 8, 1408, 2816
QT = 8
ENGINES = ("sync", "vector", "tensor", "scalar", "gpsimd")


def own_blocks(par):
    return [0, 2, 4, 6, 9, 11, 13, 15] if par == 0 else [1, 3, 5, 7, 8, 10, 12, 14]


def dslice(ap, off, dims):
    return bass.AP(ap.tensor, ap.offset + off, dims)


class Prog:
    def __init__(self, nc):
        self.nc = nc
        self.q = {e: [] for e in ENGINES}
        self.sems = {}
        self.plan = {}

    def sem(self, stack, name):
        self.sems[name] = stack.enter_context(self.nc.semaphore(name))
        self.plan[name] = 0

    def op(self, eng, f, inc=None, n=1):
        if inc is not None:
            self.plan[inc] += n
            self.q[eng].append(
                lambda E, f=f, s=inc, n=n: f(E).then_inc(self.sems[s], n))
        else:
            self.q[eng].append(lambda E, f=f: f(E))

    def dma(self, eng, out, in_, inc=None):
        if inc is None:
            inc = "ld" if eng == "sync" else "st"
        self.op(eng, lambda E, o=out, i=in_: E.dma_start(out=o, in_=i),
                inc=inc, n=16)

    def wait(self, eng, sem, val=None):
        v = self.plan[sem] if val is None else val
        if v > 0:
            self.q[eng].append(lambda E, s=sem, v=v: E.wait_ge(self.sems[s], v))

    def mark(self, sem):
        return self.plan[sem]

    def vop(self, f):
        self.op("vector", f, inc="vec")
        self.wait("vector", "vec")

    def barrier(self):
        marks = {}
        for e in ENGINES:
            s = "pr_" + e
            if e in ("sync", "tensor"):
                self.q[e].append(lambda E, s=s: E.sem_inc(self.sems[s], 1))
                self.plan[s] += 1
            elif e == "gpsimd":
                self.op(e, self.flush_gps, inc=s, n=16)
            else:
                self.op(e, self.flush_v if e == "vector" else self.flush_s,
                        inc=s)
            marks[e] = self.plan[s]
        dmamarks = {d: self.plan[d]
                    for d in ("st", "ld", "gps", "l0", "lwm", "lws", "lx0",
                              "lx1", "lq3", "ls4", "la4", "lw4", "lp4",
                              "lx5", "lm0", "lm1", "lda", "ldb")
                    if d in self.plan}
        for e in ENGINES:
            for o in ENGINES:
                if o != e:
                    self.wait(e, "pr_" + o, marks[o])
            for d, v in dmamarks.items():
                self.wait(e, d, v)

    def finalize(self, block):
        for e in ENGINES:
            ops = self.q[e]

            def run(E, ops=ops):
                for f in ops:
                    f(E)
            getattr(block, e)(run)


def build_program():
    nc = bass.Bass("TRN2", target_bir_lowering=False)

    def din(name, shape, dt=F32):
        return nc.dram_tensor(name, shape, dt, kind="ExternalInput").ap()

    x_own = din("x_own", [2048, C_])
    x_oth = din("x_oth", [2048, C_])
    wqkv = din("wqkv", [3 * C_, C_])
    bqkv_fmv = din("bqkv_fmv", [128, 24])
    bqkv_vrow = din("bqkv_vrow", [1, C_])
    wproj = din("wproj", [C_, C_])
    bproj_r = din("bproj_r", [1, C_])
    gate_w_in = din("gate_w_in", [E_, C_])
    w_gate = din("w_gate", [E_, C_, F_])
    w_up = din("w_up", [E_, C_, F_])
    w_down = din("w_down", [E_, F_, C_])
    sw_gate = din("sw_gate", [C_, FS_])
    sw_up = din("sw_up", [C_, FS_])
    sw_down = din("sw_down", [FS_, C_])
    c_row_in = din("c_row_in", [1, C_])
    wmod = din("wmod", [C_, 6 * C_])
    ident_in = din("ident_in", [128, 128])
    tri_in = din("tri_in", [128, 512])
    kill_in = din("kill_in", [128, QT])
    out_hbm = nc.dram_tensor("out", [2048, C_], F32, kind="ExternalOutput").ap()

    attn_fm = nc.dram_tensor("attn_fm", [C_, 2048], F32).ap()
    x1_s = nc.dram_tensor("x1_s", [2048, C_], F32).ap()
    sig_dram = nc.dram_tensor("sig_dram", [16, 2048], F32).ap()
    cc_in = nc.dram_tensor("cc_in", [1, 1], F32)
    cc_out = nc.dram_tensor("cc_out", [1, 1], F32)
    cc_in2 = nc.dram_tensor("cc_in2", [1, 1], F32)
    cc_out2 = nc.dram_tensor("cc_out2", [1, 1], F32)

    with ExitStack() as ES:
        def sb(name, shape, dt=F32, st=ES):
            return st.enter_context(nc.sbuf_tensor(name, shape, dt))

        def psm(name, shape, dt=F32, st=ES):
            return st.enter_context(nc.psum_tensor(name, shape, dt))

        p = Prog(nc)
        for s in (["ld", "st", "vec", "ten", "sca", "gps", "l0", "lwm",
                   "lws", "lx0", "lx1", "lq3", "ls4", "la4", "lw4", "lp4",
                   "lx5", "lm0", "lm1", "lda", "ldb"] +
                  ["pr_" + e for e in ENGINES]):
            p.sem(ES, s)

        ident = sb("ident", [128, 128])
        ident_bf = sb("ident_bf", [128, 128], BF16)
        tri = sb("tri", [128, 512])
        killc = sb("killc", [128, QT])
        ones_row = sb("ones_row", [1, 128])
        ones_col = sb("ones_col", [128, 1])
        mod_row = sb("mod_row", [1, 6 * C_])
        gwfm = sb("gwfm", [128, 8 * E_])
        bq_fm = sb("bq_fm", [128, 24])
        sc_fm = sb("sc_fm", [128, 8])
        stats = sb("stats", [128, 128])
        st1 = sb("st1", [128, 64])
        sca = sb("sca", [128, 16])
        srow = sb("srow", [1, 32])
        sq_r = sb("sq_r", [128, 16 * 16])
        rep_a = sb("rep_a", [128, C_])
        rep_b = sb("rep_b", [128, C_])
        rep_c = sb("rep_c", [128, C_])
        comb = sb("comb", [128, 64])

        def mrow(j):
            return mod_row[0:1, j * C_:(j + 1) * C_]

        flush_dram = nc.dram_tensor("flush_dram", [1, 4], F32).ap()
        p.flush_v = lambda E: E.memset(sca[0:1, 14:15], 0.0)
        p.flush_s = lambda E: E.copy(out=sca[0:1, 15:16], in_=sca[0:1, 14:15])
        p.flush_gps = lambda E: E.dma_start(out=flush_dram[0:1, 0:1],
                                            in_=sca[0:1, 14:15])

        (I_INVSX, I_SXW, I_HMAX, I_T0, I_T1, I_T2, I_NEGBA, I_NEGBB,
         I_OMAX, I_SOP, I_INVSO, I_IWQ, I_IWP) = range(13)
        (R_AQKV, R_APROJ, R_SWQ, R_SWP, R_IWQ, R_IWP, R_HLOC, R_HGLB, R_SX,
         R_ISX, R_T0, R_T1, R_OLOC, R_OGLB, R_SO, R_ISO, R_SXW, R_SOP,
         R_KA, R_KB, R_QA, R_QB) = range(22)

        def col2row_max(col_ap, dst_off, bank):
            p.wait("tensor", "vec")
            p.op("tensor", lambda E, c=col_ap, bank=bank: E.matmul(
                bank[0:1, 0:128], c, ident[:], start=True, stop=True),
                inc="ten")
            p.wait("vector", "ten")
            p.vop(lambda E, d=dst_off, bank=bank: E.tensor_reduce(
                out=srow[0:1, d:d + 1], in_=bank[0:1, 0:128], axis=AX.X,
                op=ALU.max))

        def scalar2col(row_off, dst_col, bank):
            p.wait("tensor", "vec")
            p.op("tensor", lambda E, bank=bank, r=row_off: E.matmul(
                bank[:, 0:1], ones_row[:], srow[0:1, r:r + 1],
                start=True, stop=True), inc="ten")
            p.wait("vector", "ten")
            p.vop(lambda E, d=dst_col, bank=bank: E.tensor_copy(
                out=sca[:, d:d + 1], in_=bank[:, 0:1]))

        def row2rep(row_ap, dst, bank, n=C_):
            for j in range(n // 512):
                p.wait("tensor", "vec")
                p.op("tensor", lambda E, j=j, r=row_ap, bank=bank: E.matmul(
                    bank[:, 0:512], ones_row[:],
                    r[0:1, j * 512:(j + 1) * 512],
                    start=True, stop=True), inc="ten")
                p.wait("vector", "ten")
                p.op("vector", lambda E, j=j, dst=dst, bank=bank: E.tensor_copy(
                    out=dst[:, j * 512:(j + 1) * 512], in_=bank[:, 0:512]),
                    inc="vec")

        def quant3(src_ap, dst_ap, inv_col):
            p.op("vector", lambda E: E.tensor_scalar(
                out=src_ap, in0=src_ap, scalar1=sca[:, inv_col:inv_col + 1],
                scalar2=MAGIC, op0=ALU.mult, op1=ALU.add))
            p.op("vector", lambda E: E.tensor_scalar(
                out=src_ap, in0=src_ap, scalar1=-MAGIC, scalar2=127.0,
                op0=ALU.add, op1=ALU.min))
            p.op("vector", lambda E: E.tensor_scalar(
                out=dst_ap, in0=src_ap, scalar1=-127.0, scalar2=None,
                op0=ALU.max), inc="vec")

        with ExitStack() as S0:
            wstr0 = sb("p0_wstr", [128, 1024], st=S0)
            crow = sb("p0_crow", [1, 3 * C_], st=S0)
            gwsb = sb("p0_gw", [E_, C_], st=S0)
            pb0 = [psm(f"p0_pb{i}", [128, 512], st=S0) for i in range(4)]

            for dst, src in ((ident[:], ident_in), (tri[:], tri_in),
                             (killc[:], kill_in), (crow[0:1, 0:C_], c_row_in),
                             (bq_fm[:], bqkv_fmv), (gwsb[:], gate_w_in)):
                p.dma("sync", dst, src, inc="l0")
            p.wait("vector", "l0")
            p.op("vector", lambda E: E.tensor_copy(out=ident_bf[:], in_=ident[:]))
            p.op("vector", lambda E: E.memset(ones_row[:], 1.0))
            p.op("vector", lambda E: E.memset(ones_col[:], 1.0), inc="vec")
            p.wait("scalar", "l0")
            p.op("scalar", lambda E: E.activation(
                out=crow[0:1, C_:2 * C_], in_=crow[0:1, 0:C_], func=AF.Silu),
                inc="sca")
            p.wait("tensor", "sca")
            p.wait("tensor", "vec")
            for k in range(8):
                p.op("tensor", lambda E, k=k: E.matmul(
                    pb0[0][:, k:k + 1],
                    crow[0:1, C_ + 128 * k:C_ + 128 * (k + 1)],
                    ones_row[0:1, 0:1], start=True, stop=True))
            for k in range(8):
                p.op("tensor", lambda E, k=k: E.transpose(
                    pb0[1][:, k * 8:(k + 1) * 8],
                    gwsb[0:E_, 128 * k:128 * (k + 1)], ident[0:E_, 0:E_]),
                    inc="ten" if k == 7 else None)
            p.wait("vector", "ten")
            p.op("vector", lambda E: E.tensor_copy(out=sc_fm[:], in_=pb0[0][:, 0:8]))
            p.op("vector", lambda E: E.tensor_copy(out=gwfm[:], in_=pb0[1][:, 0:64]),
                 inc="vec")
            p.wait("tensor", "vec")

            for m in range(12):
                for k in range(8):
                    p.dma("sync", wstr0[:, 0:512],
                          wmod[k * 128:(k + 1) * 128, m * 512:(m + 1) * 512],
                          inc="lwm")
                    p.wait("tensor", "lwm")
                    p.op("tensor", lambda E, k=k: E.matmul(
                        pb0[2][0:1, 0:512], sc_fm[:, k:k + 1], wstr0[:, 0:512],
                        start=(k == 0), stop=(k == 7)),
                        inc="ten")
                    p.wait("sync", "ten", p.mark("ten"))
                p.wait("scalar", "ten")
                p.op("scalar", lambda E, m=m: E.copy(
                    out=mod_row[0:1, m * 512:(m + 1) * 512],
                    in_=pb0[2][0:1, 0:512]), inc="sca")
                p.wait("tensor", "sca")

            p.wait("vector", "sca")
            p.op("vector", lambda E: E.memset(sca[:, I_T0:I_T0 + 1], 0.0),
                 inc="vec")
            for i in range(32):
                src, r0 = (wqkv, i * 128) if i < 24 else (wproj, (i - 24) * 128)
                p.wait("sync", "vec", p.mark("vec"))
                p.dma("sync", wstr0[:, 0:1024], src[r0:r0 + 128, :],
                      inc="lws")
                p.wait("vector", "lws")
                if i == 24:
                    p.vop(lambda E: E.tensor_copy(
                        out=sca[:, I_T1:I_T1 + 1], in_=sca[:, I_T0:I_T0 + 1]))
                    p.vop(lambda E: E.memset(sca[:, I_T0:I_T0 + 1], 0.0))
                p.vop(lambda E: E.tensor_reduce(
                    out=sca[:, I_T2:I_T2 + 1], in_=wstr0[:, 0:1024], axis=AX.X,
                    op=ALU.max, apply_absolute_value=True))
                p.vop(lambda E: E.tensor_tensor(
                    out=sca[:, I_T0:I_T0 + 1], in0=sca[:, I_T0:I_T0 + 1],
                    in1=sca[:, I_T2:I_T2 + 1], op=ALU.max))
            col2row_max(sca[:, I_T1:I_T1 + 1], R_AQKV, pb0[3])
            col2row_max(sca[:, I_T0:I_T0 + 1], R_APROJ, pb0[3])
            for rs, rd, ri in ((R_AQKV, R_SWQ, R_IWQ), (R_APROJ, R_SWP, R_IWP)):
                p.vop(lambda E, rs=rs, rd=rd: E.tensor_scalar(
                    out=srow[0:1, rd:rd + 1], in0=srow[0:1, rs:rs + 1],
                    scalar1=1.0 / 127.0, scalar2=1e-8, op0=ALU.mult,
                    op1=ALU.max))
                p.vop(lambda E, rd=rd, ri=ri: E.reciprocal(
                    out=srow[0:1, ri:ri + 1], in_=srow[0:1, rd:rd + 1]))
            scalar2col(R_IWQ, I_IWQ, pb0[3])
            scalar2col(R_IWP, I_IWP, pb0[3])


            p.op("vector", lambda E: E.tensor_scalar(
                out=mrow(1), in0=mrow(1), scalar1=1.0, scalar2=None,
                op0=ALU.add))
            p.op("vector", lambda E: E.tensor_scalar(
                out=mrow(4), in0=mrow(4), scalar1=1.0, scalar2=None,
                op0=ALU.add))
            for j in (2, 5):
                p.op("vector", lambda E, j=j: E.tensor_scalar(
                    out=rep_c[0:1, 0:C_], in0=mrow(j), scalar1=0.25,
                    scalar2=None, op0=ALU.max))
                p.op("vector", lambda E, j=j: E.tensor_scalar(
                    out=rep_b[0:1, 0:C_], in0=mrow(j), scalar1=-0.25,
                    scalar2=None, op0=ALU.min))
                p.op("vector", lambda E, j=j: E.tensor_scalar(
                    out=rep_a[0:1, 0:C_], in0=mrow(j), scalar1=0.0,
                    scalar2=None, op0=ALU.is_ge))
                p.op("vector", lambda E: E.tensor_tensor(
                    out=rep_c[0:1, 0:C_], in0=rep_c[0:1, 0:C_],
                    in1=rep_b[0:1, 0:C_], op=ALU.subtract))
                p.op("vector", lambda E: E.tensor_tensor(
                    out=rep_c[0:1, 0:C_], in0=rep_c[0:1, 0:C_],
                    in1=rep_a[0:1, 0:C_], op=ALU.mult))
                p.op("vector", lambda E, j=j: E.tensor_tensor(
                    out=mrow(j), in0=rep_c[0:1, 0:C_],
                    in1=rep_b[0:1, 0:C_], op=ALU.add), inc="vec")

        with ExitStack() as S1:
            xq_own = sb("xq_own", [128, 8 * 2048], BF16, st=S1)
            xq_oth = sb("xq_oth", [128, 8 * 2048], BF16, st=S1)

            with ExitStack() as S1a:
                xb1b = sb("p1_xb", [128, 2 * 1024], st=S1a)
                hb1 = sb("p1_hb", [128, 2048], st=S1a)
                qb1 = sb("p1_qb", [128, 1024], BF16, st=S1a)
                pbf1 = psm("p1_pbf", [128, 1024], BF16, st=S1a)
                pb1 = [psm(f"p1_pb{i}", [128, 512], st=S1a) for i in range(2)]

                row2rep(mrow(1), rep_a, pb1[0])
                row2rep(mod_row[0:1, 0:C_], rep_b, pb1[0])
                p.op("vector", lambda E: E.memset(sca[:, I_HMAX:I_HMAX + 1],
                                                  0.0), inc="vec")

                def ln_stats(i, xb, sums, dst_negmu, dst_invsig):
                    p.op("vector", lambda E, xb=xb: E.tensor_scalar(
                        out=hb1[:, 0:1024], in0=xb, scalar1=1.0, scalar2=None,
                        op0=ALU.mult, op1=ALU.add, accum_out=sums[0]),
                        inc="vec")
                    p.wait("vector", "vec")
                    p.op("scalar", lambda E, xb=xb: E.activation(
                        out=hb1[:, 1024:2048], in_=xb, func=AF.Square,
                        accum_out=sums[1]), inc="sca")
                    p.wait("vector", "sca")
                    p.vop(lambda E: E.tensor_scalar(
                        out=dst_negmu, in0=sums[0], scalar1=-1.0 / C_,
                        scalar2=None, op0=ALU.mult))
                    p.vop(lambda E: E.tensor_scalar(
                        out=sca[:, I_T0:I_T0 + 1], in0=sums[1],
                        scalar1=1.0 / C_, scalar2=None, op0=ALU.mult))
                    p.vop(lambda E: E.tensor_tensor(
                        out=sca[:, I_T1:I_T1 + 1], in0=dst_negmu,
                        in1=dst_negmu, op=ALU.mult))
                    p.vop(lambda E: E.tensor_tensor(
                        out=sca[:, I_T0:I_T0 + 1], in0=sca[:, I_T0:I_T0 + 1],
                        in1=sca[:, I_T1:I_T1 + 1], op=ALU.subtract))
                    p.vop(lambda E: E.tensor_scalar(
                        out=sca[:, I_T0:I_T0 + 1], in0=sca[:, I_T0:I_T0 + 1],
                        scalar1=1e-5, scalar2=None, op0=ALU.add))
                    p.wait("scalar", "vec")
                    p.op("scalar", lambda E: E.activation(
                        out=sca[:, I_T1:I_T1 + 1], in_=sca[:, I_T0:I_T0 + 1],
                        func=AF.Sqrt), inc="sca")
                    p.wait("vector", "sca")
                    p.vop(lambda E: E.reciprocal(
                        out=sca[:, I_T2:I_T2 + 1], in_=sca[:, I_T1:I_T1 + 1]))
                    p.vop(lambda E: E.tensor_tensor(
                        out=sca[:, I_T1:I_T1 + 1], in0=sca[:, I_T2:I_T2 + 1],
                        in1=sca[:, I_T2:I_T2 + 1], op=ALU.mult))
                    p.vop(lambda E: E.tensor_tensor(
                        out=sca[:, I_T1:I_T1 + 1], in0=sca[:, I_T1:I_T1 + 1],
                        in1=sca[:, I_T0:I_T0 + 1], op=ALU.mult))
                    p.vop(lambda E: E.tensor_scalar(
                        out=sca[:, I_T1:I_T1 + 1], in0=sca[:, I_T1:I_T1 + 1],
                        scalar1=-0.5, scalar2=1.5, op0=ALU.mult, op1=ALU.add))
                    p.vop(lambda E: E.tensor_tensor(
                        out=dst_invsig, in0=sca[:, I_T2:I_T2 + 1],
                        in1=sca[:, I_T1:I_T1 + 1], op=ALU.mult))

                for i in range(32):
                    src = x_own if i < 16 else x_oth
                    t = i % 16
                    xb = xb1b[:, (i % 2) * 1024:(i % 2 + 1) * 1024]
                    if i >= 2:
                        p.wait("sync", "vec", p.mark("vec"))
                    lxs = "lx0" if i % 2 == 0 else "lx1"
                    p.dma("sync", xb, src[t * 128:(t + 1) * 128, :], inc=lxs)
                    p.wait("vector", lxs)
                    p.wait("scalar", lxs)
                    ln_stats(i, xb,
                             (stats[:, i:i + 1], stats[:, 32 + i:33 + i]),
                             stats[:, 64 + i:65 + i], stats[:, 96 + i:97 + i])
                    p.op("vector", lambda E, xb=xb, i=i: E.tensor_scalar(
                        out=hb1[:, 0:1024], in0=xb,
                        scalar1=stats[:, 64 + i:65 + i],
                        scalar2=stats[:, 96 + i:97 + i], op0=ALU.add,
                        op1=ALU.mult))
                    p.op("vector", lambda E: E.tensor_tensor(
                        out=hb1[:, 0:1024], in0=hb1[:, 0:1024], in1=rep_a[:],
                        op=ALU.mult))
                    p.op("vector", lambda E: E.tensor_tensor(
                        out=hb1[:, 0:1024], in0=hb1[:, 0:1024], in1=rep_b[:],
                        op=ALU.add))
                    p.vop(lambda E: E.tensor_reduce(
                        out=sca[:, I_T0:I_T0 + 1], in_=hb1[:, 0:1024],
                        axis=AX.X, op=ALU.max, apply_absolute_value=True))
                    p.vop(lambda E: E.tensor_tensor(
                        out=sca[:, I_HMAX:I_HMAX + 1],
                        in0=sca[:, I_HMAX:I_HMAX + 1],
                        in1=sca[:, I_T0:I_T0 + 1], op=ALU.max))

                col2row_max(sca[:, I_HMAX:I_HMAX + 1], R_HLOC, pb1[0])
                p.wait("gpsimd", "vec")
                p.dma("gpsimd", cc_in.ap(), srow[0:1, R_HLOC:R_HLOC + 1])
                p.wait("gpsimd", "st")
                p.op("gpsimd", lambda E: E.collective_compute(
                    "AllReduce", ALU.max, replica_groups=[list(range(8))],
                    ins=[cc_in.ap().opt()], outs=[cc_out.ap().opt()]),
                    inc="gps")
                p.wait("gpsimd", "gps")
                p.dma("gpsimd", srow[0:1, R_HGLB:R_HGLB + 1], cc_out.ap())
                p.wait("vector", "st")
                p.vop(lambda E: E.tensor_scalar(
                    out=srow[0:1, R_SX:R_SX + 1],
                    in0=srow[0:1, R_HGLB:R_HGLB + 1], scalar1=1.0 / 127.0,
                    scalar2=1e-8, op0=ALU.mult, op1=ALU.max))
                p.vop(lambda E: E.reciprocal(
                    out=srow[0:1, R_ISX:R_ISX + 1],
                    in_=srow[0:1, R_SX:R_SX + 1]))
                p.vop(lambda E: E.tensor_tensor(
                    out=srow[0:1, R_SXW:R_SXW + 1],
                    in0=srow[0:1, R_SX:R_SX + 1],
                    in1=srow[0:1, R_SWQ:R_SWQ + 1], op=ALU.mult))
                scalar2col(R_ISX, I_INVSX, pb1[0])
                scalar2col(R_SXW, I_SXW, pb1[0])
                p.op("vector", lambda E: E.tensor_scalar(
                    out=rep_a[:], in0=rep_a[:],
                    scalar1=sca[:, I_INVSX:I_INVSX + 1], scalar2=None,
                    op0=ALU.mult))
                p.op("vector", lambda E: E.tensor_scalar(
                    out=rep_b[:], in0=rep_b[:],
                    scalar1=sca[:, I_INVSX:I_INVSX + 1], scalar2=None,
                    op0=ALU.mult), inc="vec")

                for i in range(32):
                    src = x_own if i < 16 else x_oth
                    dst = xq_own if i < 16 else xq_oth
                    t = i % 16
                    xb = xb1b[:, (i % 2) * 1024:(i % 2 + 1) * 1024]
                    p.wait("sync", "vec", p.mark("vec"))
                    lxs = "lx0" if i % 2 == 0 else "lx1"
                    p.dma("sync", xb, src[t * 128:(t + 1) * 128, :], inc=lxs)
                    p.wait("vector", lxs)
                    p.op("vector", lambda E, xb=xb, i=i: E.tensor_scalar(
                        out=hb1[:, 0:1024], in0=xb,
                        scalar1=stats[:, 64 + i:65 + i],
                        scalar2=stats[:, 96 + i:97 + i], op0=ALU.add,
                        op1=ALU.mult))
                    p.op("vector", lambda E: E.tensor_tensor(
                        out=hb1[:, 0:1024], in0=hb1[:, 0:1024], in1=rep_a[:],
                        op=ALU.mult))
                    p.op("vector", lambda E: E.tensor_tensor(
                        out=hb1[:, 0:1024], in0=hb1[:, 0:1024], in1=rep_b[:],
                        op=ALU.add))
                    p.op("vector", lambda E: E.tensor_scalar(
                        out=hb1[:, 0:1024], in0=hb1[:, 0:1024], scalar1=MAGIC,
                        scalar2=None, op0=ALU.add))
                    p.op("vector", lambda E: E.tensor_scalar(
                        out=hb1[:, 0:1024], in0=hb1[:, 0:1024], scalar1=-MAGIC,
                        scalar2=127.0, op0=ALU.add, op1=ALU.min))
                    p.op("vector", lambda E: E.tensor_scalar(
                        out=qb1[:, 0:1024], in0=hb1[:, 0:1024], scalar1=-127.0,
                        scalar2=None, op0=ALU.max), inc="vec")
                    p.wait("tensor", "vec")
                    for k in range(8):
                        p.op("tensor", lambda E, k=k: E.transpose(
                            pbf1[:, k * 128:(k + 1) * 128],
                            qb1[:, k * 128:(k + 1) * 128], ident_bf[:]),
                            inc="ten" if k == 7 else None)
                    p.wait("vector", "ten")
                    for k in range(8):
                        p.op("vector",
                             lambda E, k=k, dst=dst, t=t: E.tensor_copy(
                                 out=dst[:, k * 2048 + t * 128:
                                         k * 2048 + (t + 1) * 128],
                                 in_=pbf1[:, k * 128:(k + 1) * 128]),
                             inc="vec" if k == 7 else None)
                    p.wait("tensor", "vec")

            p.barrier()

            with ExitStack() as S3:
                kpair = sb("kpair", [128, 4096], st=S3)
                nba = sb("nba", [1, 2048], BF16, st=S3)
                nbb = sb("nbb", [1, 2048], BF16, st=S3)
                ones_row_bf = sb("ones_row_bf", [1, 128], BF16, st=S3)
                ob3 = sb("ob3", [128, 512], st=S3)
                qpair = sb("qpair", [128, 2048], st=S3)
                vpair = sb("vpair", [128, 32 * 130], st=S3)
                e_a = sb("e_a", [128, 8 * 256], st=S3)
                e_b = sb("e_b", [128, 8 * 256], st=S3)
                wstr3 = sb("p3_wstr", [128, 1024], st=S3)
                wqq3 = sb("p3_wqq", [128, 1024], BF16, st=S3)
                wq_fm = sb("wq_fm", [128, 8 * 256], BF16, st=S3)
                wv_fm = sb("wv_fm", [128, 8 * 128], BF16, st=S3)
                brep = sb("p3_brep", [128, 128], st=S3)
                k2b = sb("p3_k2", [128, 512], st=S3)
                pbf3 = psm("p3_pbf", [128, 1024], BF16, st=S3)
                pS = [psm(f"p3_ps{i}", [128, 512], st=S3) for i in range(4)]
                pO = [psm(f"p3_po{i}", [128, 512], st=S3) for i in range(2)]
                pX = psm("p3_px", [128, 512], st=S3)

                def stream_quant_w(rows0, dst_fm, sec, secoff):
                    p.wait("sync", "vec", p.mark("vec"))
                    p.dma("sync", wstr3[:, 0:1024], wqkv[rows0:rows0 + 128, :],
                          inc="lq3")
                    p.wait("vector", "lq3")
                    quant3(wstr3[:, 0:1024], wqq3[:, 0:1024], I_IWQ)
                    p.wait("tensor", "vec")
                    for k in range(8):
                        p.op("tensor", lambda E, k=k: E.transpose(
                            pbf3[:, k * 128:(k + 1) * 128],
                            wqq3[:, k * 128:(k + 1) * 128], ident_bf[:]),
                            inc="ten" if k == 7 else None)
                    p.wait("vector", "ten")
                    for k in range(8):
                        p.op("vector", lambda E, k=k: E.tensor_copy(
                            out=dst_fm[:, k * sec + secoff:k * sec + secoff + 128],
                            in_=pbf3[:, k * 128:(k + 1) * 128]),
                            inc="vec" if k == 7 else None)
                    p.wait("tensor", "vec")

                p.op("vector", lambda E: E.memset(ones_row_bf[:], 1.0),
                     inc="vec")
                for pair in range(8):
                    stream_quant_w(C_ + 128 * pair, wq_fm, 256, 0)
                    stream_quant_w(128 * pair, wq_fm, 256, 128)
                    stream_quant_w(2 * C_ + 128 * pair, wv_fm, 128, 0)

                    p.op("vector", lambda E: E.memset(srow[0:1, R_KA:R_KA + 2],
                                                      0.0), inc="vec")
                    for tc in range(8):
                        xsrc = xq_own if tc < 4 else xq_oth
                        tof = (tc % 4) * 512
                        p.wait("tensor", "vec")
                        for ic in range(8):
                            p.op("tensor", lambda E, ic=ic, xsrc=xsrc, tof=tof:
                                 E.matmul(
                                     pX[:, 0:512],
                                     wq_fm[:, ic * 256:ic * 256 + 128],
                                     xsrc[:, ic * 2048 + tof:
                                          ic * 2048 + tof + 512],
                                     start=(ic == 0), stop=(ic == 7)),
                                 inc="ten" if ic == 7 else None)
                        p.wait("vector", "ten")
                        for hh in range(2):
                            bfree = dslice(bq_fm.ap(),
                                           (8 + pair) + 64 * hh * 24,
                                           [[24, 64], [0, 512]])
                            p.op("vector", lambda E, tc=tc, hh=hh, bfree=bfree:
                                 E.scalar_tensor_tensor(
                                     out=kpair[64 * hh:64 * hh + 64,
                                               tc * 512:(tc + 1) * 512],
                                     in0=pX[64 * hh:64 * hh + 64, 0:512],
                                     scalar=sca[64 * hh:64 * hh + 64,
                                                I_SXW:I_SXW + 1],
                                     in1=bfree, op0=ALU.mult, op1=ALU.add))
                        p.op("vector", lambda E, tc=tc: E.tensor_tensor(
                            out=k2b[:, 0:512],
                            in0=kpair[:, tc * 512:(tc + 1) * 512],
                            in1=kpair[:, tc * 512:(tc + 1) * 512],
                            op=ALU.mult), inc="vec")
                        p.wait("tensor", "vec")
                        for hh in range(2):
                            p.op("tensor", lambda E, hh=hh: E.matmul(
                                pS[hh][0:1, 0:512],
                                ones_col[64 * hh:64 * hh + 64, 0:1],
                                k2b[64 * hh:64 * hh + 64, 0:512],
                                start=True, stop=True,
                                tile_position=(64 * hh, 0)),
                                inc="ten" if hh == 1 else None)
                        p.wait("vector", "ten")
                        for hh in range(2):
                            p.vop(lambda E, hh=hh: E.tensor_reduce(
                                out=srow[0:1, R_T0 + hh:R_T0 + hh + 1],
                                in_=pS[hh][0:1, 0:512], axis=AX.X, op=ALU.max))
                            p.vop(lambda E, hh=hh: E.tensor_tensor(
                                out=srow[0:1, R_KA + hh:R_KA + hh + 1],
                                in0=srow[0:1, R_KA + hh:R_KA + hh + 1],
                                in1=srow[0:1, R_T0 + hh:R_T0 + hh + 1],
                                op=ALU.max))
                        p.wait("tensor", "vec")

                    p.op("vector", lambda E: E.memset(srow[0:1, R_QA:R_QA + 2],
                                                      0.0), inc="vec")
                    for tc in range(4):
                        tof = tc * 512
                        p.wait("tensor", "vec")
                        for ic in range(8):
                            p.op("tensor", lambda E, ic=ic, tof=tof: E.matmul(
                                pX[:, 0:512],
                                wq_fm[:, ic * 256 + 128:ic * 256 + 256],
                                xq_own[:, ic * 2048 + tof:ic * 2048 + tof + 512],
                                start=(ic == 0), stop=(ic == 7)),
                                inc="ten" if ic == 7 else None)
                        p.wait("vector", "ten")
                        for hh in range(2):
                            bfree = dslice(bq_fm.ap(),
                                           pair + 64 * hh * 24,
                                           [[24, 64], [0, 512]])
                            p.op("vector", lambda E, tc=tc, hh=hh, bfree=bfree:
                                 E.scalar_tensor_tensor(
                                     out=qpair[64 * hh:64 * hh + 64,
                                               tc * 512:(tc + 1) * 512],
                                     in0=pX[64 * hh:64 * hh + 64, 0:512],
                                     scalar=sca[64 * hh:64 * hh + 64,
                                                I_SXW:I_SXW + 1],
                                     in1=bfree, op0=ALU.mult, op1=ALU.add))
                        p.op("vector", lambda E, tc=tc: E.tensor_tensor(
                            out=k2b[:, 0:512],
                            in0=qpair[:, tc * 512:(tc + 1) * 512],
                            in1=qpair[:, tc * 512:(tc + 1) * 512],
                            op=ALU.mult), inc="vec")
                        p.wait("tensor", "vec")
                        for hh in range(2):
                            p.op("tensor", lambda E, hh=hh: E.matmul(
                                pS[hh][0:1, 0:512],
                                ones_col[64 * hh:64 * hh + 64, 0:1],
                                k2b[64 * hh:64 * hh + 64, 0:512],
                                start=True, stop=True,
                                tile_position=(64 * hh, 0)),
                                inc="ten" if hh == 1 else None)
                        p.wait("vector", "ten")
                        for hh in range(2):
                            qrow = (e_a if hh == 0 else e_b)
                            p.vop(lambda E, hh=hh, tc=tc, qrow=qrow:
                                  E.tensor_copy(
                                      out=qrow[0:1, tc * 512:(tc + 1) * 512],
                                      in_=pS[hh][0:1, 0:512]))
                        p.wait("tensor", "vec")

                    for hh, nbd in ((0, nba), (1, nbb)):
                        qrow = (e_a if hh == 0 else e_b)
                        p.vop(lambda E, hh=hh, qrow=qrow: E.tensor_scalar(
                            out=qrow[0:1, 0:2048], in0=qrow[0:1, 0:2048],
                            scalar1=srow[0:1, R_KA + hh:R_KA + hh + 1],
                            scalar2=None, op0=ALU.mult))
                        p.wait("scalar", "vec")
                        p.op("scalar", lambda E, qrow=qrow: E.activation(
                            out=qrow[0:1, 0:2048], in_=qrow[0:1, 0:2048],
                            func=AF.Sqrt), inc="sca")
                        p.wait("vector", "sca")
                        p.vop(lambda E, nbd=nbd, qrow=qrow: E.tensor_scalar(
                            out=nbd[0:1, :], in0=qrow[0:1, 0:2048],
                            scalar1=-0.5, scalar2=None, op0=ALU.mult))

                    for tc4 in range(8):
                        xsrc = xq_own if tc4 < 4 else xq_oth
                        tof = (tc4 % 4) * 512
                        p.wait("tensor", "vec")
                        p.wait("tensor", "st", p.mark("st"))
                        for ic in range(8):
                            p.op("tensor", lambda E, ic=ic, xsrc=xsrc, tof=tof:
                                 E.matmul(
                                     pX[:, 0:512],
                                     wv_fm[:, ic * 128:(ic + 1) * 128],
                                     xsrc[:, ic * 2048 + tof:
                                          ic * 2048 + tof + 512],
                                     start=(ic == 0), stop=(ic == 7)),
                                 inc="ten" if ic == 7 else None)
                        p.wait("vector", "ten")
                        vb = dslice(bq_fm.ap(), 16 + pair,
                                    [[24, 128], [0, 512]])
                        p.op("vector", lambda E, vb=vb: E.scalar_tensor_tensor(
                            out=k2b[:, 0:512], in0=pX[:, 0:512],
                            scalar=sca[:, I_SXW:I_SXW + 1], in1=vb,
                            op0=ALU.mult, op1=ALU.add), inc="vec")
                        p.wait("tensor", "vec")
                        for k in range(4):
                            p.op("tensor", lambda E, k=k: E.transpose(
                                pS[k][:, 0:128],
                                k2b[:, k * 128:(k + 1) * 128], ident[:]),
                                inc="ten" if k == 3 else None)
                        p.wait("vector", "ten")
                        for k in range(4):
                            tc = tc4 * 4 + k
                            p.op("vector", lambda E, k=k, tc=tc: E.tensor_copy(
                                out=dslice(vpair.ap(), tc * 130,
                                           [[32 * 130, 128], [65, 2],
                                            [1, 64]]),
                                in_=dslice(pS[k].ap(), 0,
                                           [[512, 128], [64, 2], [1, 64]])))
                            p.op("vector", lambda E, tc=tc: E.memset(
                                dslice(vpair.ap(), tc * 130 + 64,
                                       [[32 * 130, 128], [65, 2], [1, 1]]),
                                1.0), inc="vec" if k == 3 else None)
                        p.wait("tensor", "vec")

                    for qt in range(QT):
                        nslots = 4 * qt + 4
                        def slot_info(s):
                            if s < 2 * qt + 2:
                                kcol, vch = s * 128, s
                                own, j = True, s
                            else:
                                j = s - (2 * qt + 2)
                                kcol, vch = 2048 + j * 128, 16 + j
                                own = False
                            return kcol, vch, own, j

                        for half in range(4):
                            lo = half * 8
                            hi = min(nslots, lo + 8)
                            if lo >= hi:
                                continue
                            e0h = p.mark("sca")
                            for s in range(lo, hi):
                                kcol, vch, ownk, j = slot_info(s)
                                bi = (s % 2) * 2
                                p.wait("tensor", "sca", p.mark("sca") - 1)
                                for hh in range(2):
                                    p.op("tensor",
                                         lambda E, hh=hh, kcol=kcol, bi=bi,
                                         qt=qt: E.matmul(
                                             pS[bi + hh][:, 0:256],
                                             kpair[64 * hh:64 * hh + 64,
                                                   kcol:kcol + 128],
                                             qpair[64 * hh:64 * hh + 64,
                                                   qt * 256:(qt + 1) * 256],
                                             start=True, stop=False,
                                             tile_position=(64 * hh, 0)))
                                for hh, nbd in ((0, nba), (1, nbb)):
                                    p.op("tensor",
                                         lambda E, hh=hh, bi=bi, qt=qt,
                                         nbd=nbd: E.matmul(
                                             pS[bi + hh][:, 0:256],
                                             ones_row_bf[0:1, 0:128],
                                             nbd[0:1,
                                                 qt * 256:(qt + 1) * 256],
                                             start=False, stop=True),
                                         inc="ten" if hh == 1 else None)
                                if ownk and j >= 2 * qt:
                                    toff = (j - 2 * qt) * 256
                                    p.wait("vector", "ten")
                                    for hh in range(2):
                                        p.op("vector",
                                             lambda E, hh=hh, bi=bi, toff=toff:
                                             E.tensor_tensor(
                                                 out=pS[bi + hh][:, 0:256],
                                                 in0=pS[bi + hh][:, 0:256],
                                                 in1=tri[:, toff:toff + 256],
                                                 op=ALU.add),
                                             inc="vec" if hh == 1 else None)
                                    p.wait("scalar", "vec")
                                elif (not ownk) and j >= 2 * qt:
                                    p.wait("vector", "ten")
                                    for hh in range(2):
                                        p.op("vector",
                                             lambda E, hh=hh, bi=bi, qt=qt:
                                             E.tensor_scalar(
                                                 out=pS[bi + hh][:, 0:256],
                                                 in0=pS[bi + hh][:, 0:256],
                                                 scalar1=killc[:, qt:qt + 1],
                                                 scalar2=None, op0=ALU.add),
                                             inc="vec" if hh == 1 else None)
                                    p.wait("scalar", "vec")
                                else:
                                    p.wait("scalar", "ten")
                                for hh, eb in ((0, e_a), (1, e_b)):
                                    p.op("scalar",
                                         lambda E, hh=hh, eb=eb,
                                         s=s, bi=bi: E.activation(
                                             out=eb[:, (s % 8) * 256:
                                                (s % 8 + 1) * 256],
                                             in_=pS[bi + hh][:, 0:256],
                                             func=AF.Exp,
                                             scale=0.125),
                                         inc="sca" if hh == 1 else None)
                            for s in range(lo, hi):
                                kcol, vch, ownk, j = slot_info(s)
                                first = (s == 0)
                                last = (s == nslots - 1)
                                p.wait("tensor", "sca", e0h + (s - lo + 1))
                                for hh, eb in ((0, e_a), (1, e_b)):
                                    p.op("tensor",
                                         lambda E, hh=hh, eb=eb, vch=vch,
                                         s=s, first=first, last=last:
                                         E.matmul(
                                             pO[hh][0:65, 0:256],
                                             vpair[:, vch * 130 + 65 * hh:
                                                   vch * 130 + 65 * hh + 65],
                                             eb[:, (s % 8) * 256:
                                                (s % 8 + 1) * 256],
                                             start=first, stop=last),
                                         inc="ten" if (hh == 1 and last)
                                         else None)
                        p.wait("scalar", "ten")
                        p.wait("scalar", "st", p.mark("st"))
                        for hh in range(2):
                            p.op("scalar", lambda E, hh=hh: E.copy(
                                out=ob3[0:65, hh * 256:(hh + 1) * 256],
                                in_=pO[hh][0:65, 0:256]),
                                inc="sca" if hh == 1 else None)
                        p.wait("gpsimd", "sca")
                        for hh in range(2):
                            head = 2 * pair + hh
                            p.dma("gpsimd",
                                  attn_fm[64 * head:64 * head + 64,
                                          qt * 256:(qt + 1) * 256],
                                  ob3[0:64, hh * 256:(hh + 1) * 256])
                            p.dma("gpsimd",
                                  sig_dram[head:head + 1,
                                           qt * 256:(qt + 1) * 256],
                                  ob3[64:65, hh * 256:(hh + 1) * 256])
                        if qt == QT - 1:
                            p.wait("tensor", "sca", p.mark("sca"))

            p.barrier()

        with ExitStack() as S4:
            att_tm = sb("att_tm", [128, 16 * 1024], st=S4)
            oq_fm = sb("oq_fm", [128, 8 * 2048], BF16, st=S4)
            wpq_fm = sb("wpq_fm", [128, 8 * 1024], BF16, st=S4)
            wstr4 = sb("p4_wstr", [128, 1024], st=S4)
            wqq4 = sb("p4_wqq", [128, 1024], BF16, st=S4)
            abuf = sb("p4_abuf", [128, 2 * 128], st=S4)
            sgb = sb("p4_sgb", [16, 128], st=S4)
            bpr = sb("p4_bpr", [1, C_], st=S4)
            xb4 = sb("p4_xb", [128, 2 * 1024], st=S4)
            hb4 = sb("p4_hb", [128, 1024], st=S4)
            qb4 = sb("p4_qb", [128, 1024], BF16, st=S4)
            pbf4 = psm("p4_pbf", [128, 1024], BF16, st=S4)
            pb4 = [psm(f"p4_pb{i}", [128, 512], st=S4) for i in range(4)]

            for qc in range(16):
                p.wait("sync", "ten", p.mark("ten"))
                p.dma("sync", sgb[0:16, 0:128],
                      sig_dram[0:16, qc * 128:(qc + 1) * 128], inc="ls4")
                p.wait("tensor", "ls4")
                p.wait("tensor", "vec", p.mark("vec"))
                p.op("tensor", lambda E, qc=qc: E.transpose(
                    pb4[qc % 2][:, 0:16],
                    sgb[0:16, 0:128], ident[0:16, 0:16]),
                    inc="ten")
                p.wait("vector", "ten")
                p.vop(lambda E, qc=qc: E.tensor_copy(
                    out=sq_r[:, qc * 16:(qc + 1) * 16],
                    in_=pb4[qc % 2][:, 0:16]))
            p.vop(lambda E: E.reciprocal(out=sq_r[:], in_=sq_r[:]))
            p.op("vector", lambda E: E.memset(sca[:, I_OMAX:I_OMAX + 1], 0.0),
                 inc="vec")

            for qc in range(16):
                for hc in range(8):
                    buf = abuf[:, (hc % 2) * 128:(hc % 2 + 1) * 128]
                    if qc + hc > 0:
                        p.wait("sync", "ten", p.mark("ten"))
                    p.dma("sync", buf,
                          attn_fm[hc * 128:(hc + 1) * 128,
                                  qc * 128:(qc + 1) * 128], inc="la4")
                    p.wait("tensor", "la4")
                    p.wait("tensor", "vec", p.mark("vec"))
                    p.op("tensor", lambda E, buf=buf, hc=hc: E.transpose(
                        pb4[2 + hc % 2][:, 0:128], buf, ident[:]), inc="ten")
                    p.wait("vector", "ten")
                    rbc = dslice(sq_r.ap(), qc * 16 + 2 * hc,
                                 [[256, 128], [1, 2], [0, 64]])
                    p.op("vector", lambda E, qc=qc, hc=hc, rbc=rbc:
                         E.tensor_tensor(
                             out=dslice(att_tm.ap(), qc * 1024 + hc * 128,
                                    [[16 * 1024, 128], [64, 2], [1, 64]]),
                             in0=dslice(pb4[2 + hc % 2].ap(), 0,
                                        [[512, 128], [64, 2], [1, 64]]),
                             in1=rbc, op=ALU.mult), inc="vec")
                p.vop(lambda E, qc=qc: E.tensor_reduce(
                    out=sca[:, I_T0:I_T0 + 1],
                    in_=att_tm[:, qc * 1024:(qc + 1) * 1024], axis=AX.X,
                    op=ALU.max, apply_absolute_value=True))
                p.vop(lambda E: E.tensor_tensor(
                    out=sca[:, I_OMAX:I_OMAX + 1],
                    in0=sca[:, I_OMAX:I_OMAX + 1], in1=sca[:, I_T0:I_T0 + 1],
                    op=ALU.max))

            col2row_max(sca[:, I_OMAX:I_OMAX + 1], R_OLOC, pb4[0])
            p.wait("gpsimd", "vec")
            p.dma("gpsimd", cc_in2.ap(), srow[0:1, R_OLOC:R_OLOC + 1])
            p.wait("gpsimd", "st")
            p.op("gpsimd", lambda E: E.collective_compute(
                "AllReduce", ALU.max, replica_groups=[list(range(8))],
                ins=[cc_in2.ap().opt()], outs=[cc_out2.ap().opt()]),
                inc="gps")
            p.wait("gpsimd", "gps")
            p.dma("gpsimd", srow[0:1, R_OGLB:R_OGLB + 1], cc_out2.ap())
            p.wait("vector", "st")
            p.vop(lambda E: E.tensor_scalar(
                out=srow[0:1, R_SO:R_SO + 1], in0=srow[0:1, R_OGLB:R_OGLB + 1],
                scalar1=1.0 / 127.0, scalar2=1e-8, op0=ALU.mult, op1=ALU.max))
            p.vop(lambda E: E.reciprocal(
                out=srow[0:1, R_ISO:R_ISO + 1], in_=srow[0:1, R_SO:R_SO + 1]))
            p.vop(lambda E: E.tensor_tensor(
                out=srow[0:1, R_SOP:R_SOP + 1], in0=srow[0:1, R_SO:R_SO + 1],
                in1=srow[0:1, R_SWP:R_SWP + 1], op=ALU.mult))
            scalar2col(R_ISO, I_INVSO, pb4[0])
            scalar2col(R_SOP, I_SOP, pb4[0])

            for qc in range(16):
                quant3(att_tm[:, qc * 1024:(qc + 1) * 1024], qb4[:, 0:1024],
                       I_INVSO)
                p.wait("tensor", "vec")
                for k in range(8):
                    p.op("tensor", lambda E, k=k: E.transpose(
                        pbf4[:, k * 128:(k + 1) * 128],
                        qb4[:, k * 128:(k + 1) * 128], ident_bf[:]),
                        inc="ten" if k == 7 else None)
                p.wait("vector", "ten")
                for k in range(8):
                    p.op("vector", lambda E, k=k, qc=qc: E.tensor_copy(
                        out=oq_fm[:, k * 2048 + qc * 128:
                                  k * 2048 + (qc + 1) * 128],
                        in_=pbf4[:, k * 128:(k + 1) * 128]),
                        inc="vec" if k == 7 else None)
                p.wait("tensor", "vec")

            for rb in range(8):
                p.wait("sync", "vec", p.mark("vec"))
                p.dma("sync", wstr4[:, 0:1024],
                      wproj[rb * 128:(rb + 1) * 128, :], inc="lw4")
                p.wait("vector", "lw4")
                quant3(wstr4[:, 0:1024], wqq4[:, 0:1024], I_IWP)
                p.wait("tensor", "vec")
                for k in range(8):
                    p.op("tensor", lambda E, k=k: E.transpose(
                        pbf4[:, k * 128:(k + 1) * 128],
                        wqq4[:, k * 128:(k + 1) * 128], ident_bf[:]),
                        inc="ten" if k == 7 else None)
                p.wait("vector", "ten")
                for k in range(8):
                    p.op("vector", lambda E, k=k, rb=rb: E.tensor_copy(
                        out=wpq_fm[:, k * 1024 + rb * 128:
                                   k * 1024 + (rb + 1) * 128],
                        in_=pbf4[:, k * 128:(k + 1) * 128]),
                        inc="vec" if k == 7 else None)
                p.wait("tensor", "vec")

            p.dma("sync", bpr[0:1, :], bproj_r, inc="lp4")
            p.wait("tensor", "lp4")
            row2rep(mrow(2), rep_a, pb4[0])
            row2rep(bpr[0:1, :], rep_b, pb4[0])
            p.op("vector", lambda E: E.tensor_tensor(
                out=rep_b[:], in0=rep_b[:], in1=rep_a[:], op=ALU.mult))
            p.op("vector", lambda E: E.tensor_scalar(
                out=rep_a[:], in0=rep_a[:], scalar1=sca[:, I_SOP:I_SOP + 1],
                scalar2=None, op0=ALU.mult), inc="vec")

            for qc in range(16):
                xb = xb4[:, (qc % 2) * 1024:(qc % 2 + 1) * 1024]
                if qc >= 2:
                    p.wait("sync", "vec", p.mark("vec"))
                    p.wait("sync", "sca", p.mark("sca"))
                lxs = "lx0" if qc % 2 == 0 else "lx1"
                p.dma("sync", xb, x_own[qc * 128:(qc + 1) * 128, :], inc=lxs)
                p.wait("tensor", "vec", p.mark("vec"))
                p.wait("tensor", "st", p.mark("st"))
                for ic in range(8):
                    for half in range(2):
                        p.op("tensor", lambda E, ic=ic, half=half, qc=qc:
                             E.matmul(
                                 pb4[half][:, 0:512],
                                 oq_fm[:, ic * 2048 + qc * 128:
                                       ic * 2048 + (qc + 1) * 128],
                                 wpq_fm[:, ic * 1024 + half * 512:
                                        ic * 1024 + (half + 1) * 512],
                                 start=(ic == 0), stop=(ic == 7)),
                             inc="ten" if (ic == 7 and half == 1) else None)
                p.wait("vector", "ten")
                p.wait("vector", lxs)
                p.wait("vector", "st", p.mark("st"))
                for half in range(2):
                    sl = slice(half * 512, (half + 1) * 512)
                    p.op("vector", lambda E, half=half, sl=sl: E.tensor_tensor(
                        out=hb4[:, sl], in0=pb4[half][:, 0:512],
                        in1=rep_a[:, sl], op=ALU.mult))
                    p.op("vector", lambda E, half=half, sl=sl: E.tensor_tensor(
                        out=hb4[:, sl], in0=hb4[:, sl], in1=rep_b[:, sl],
                        op=ALU.add))
                    p.op("vector", lambda E, half=half, sl=sl, xb=xb:
                         E.tensor_tensor(
                             out=hb4[:, sl], in0=hb4[:, sl], in1=xb[:, sl],
                             op=ALU.add), inc="vec")
                p.op("vector", lambda E, qc=qc: E.tensor_scalar(
                    out=hb4[:, 0:1024],
                    in0=hb4[:, 0:1024], scalar1=1.0, scalar2=None,
                    op0=ALU.mult, op1=ALU.add,
                    accum_out=st1[:, qc:qc + 1]), inc="vec")
                p.wait("vector", "vec")
                p.wait("scalar", "vec")
                p.op("scalar", lambda E, qc=qc: E.activation(
                    out=xb[:, 0:1024], in_=hb4[:, 0:1024], func=AF.Square,
                    accum_out=st1[:, 16 + qc:17 + qc]), inc="sca")
                p.wait("gpsimd", "sca")
                p.dma("gpsimd", x1_s[qc * 128:(qc + 1) * 128, :],
                      hb4[:, 0:1024])
            p.wait("vector", "sca")
            for qc in range(16):
                p.vop(lambda E, qc=qc: E.tensor_scalar(
                    out=st1[:, 32 + qc:33 + qc], in0=st1[:, qc:qc + 1],
                    scalar1=-1.0 / C_, scalar2=None, op0=ALU.mult))
                p.vop(lambda E, qc=qc: E.tensor_scalar(
                    out=sca[:, I_T0:I_T0 + 1], in0=st1[:, 16 + qc:17 + qc],
                    scalar1=1.0 / C_, scalar2=None, op0=ALU.mult))
                p.vop(lambda E, qc=qc: E.tensor_tensor(
                    out=sca[:, I_T1:I_T1 + 1], in0=st1[:, 32 + qc:33 + qc],
                    in1=st1[:, 32 + qc:33 + qc], op=ALU.mult))
                p.vop(lambda E: E.tensor_tensor(
                    out=sca[:, I_T0:I_T0 + 1], in0=sca[:, I_T0:I_T0 + 1],
                    in1=sca[:, I_T1:I_T1 + 1], op=ALU.subtract))
                p.vop(lambda E: E.tensor_scalar(
                    out=sca[:, I_T0:I_T0 + 1], in0=sca[:, I_T0:I_T0 + 1],
                    scalar1=1e-5, scalar2=None, op0=ALU.add))
                p.wait("scalar", "vec")
                p.op("scalar", lambda E: E.activation(
                    out=sca[:, I_T1:I_T1 + 1], in_=sca[:, I_T0:I_T0 + 1],
                    func=AF.Sqrt), inc="sca")
                p.wait("vector", "sca")
                p.vop(lambda E: E.reciprocal(
                    out=sca[:, I_T2:I_T2 + 1], in_=sca[:, I_T1:I_T1 + 1]))
                p.vop(lambda E: E.tensor_tensor(
                    out=sca[:, I_T1:I_T1 + 1], in0=sca[:, I_T2:I_T2 + 1],
                    in1=sca[:, I_T2:I_T2 + 1], op=ALU.mult))
                p.vop(lambda E: E.tensor_tensor(
                    out=sca[:, I_T1:I_T1 + 1], in0=sca[:, I_T1:I_T1 + 1],
                    in1=sca[:, I_T0:I_T0 + 1], op=ALU.mult))
                p.vop(lambda E: E.tensor_scalar(
                    out=sca[:, I_T1:I_T1 + 1], in0=sca[:, I_T1:I_T1 + 1],
                    scalar1=-0.5, scalar2=1.5, op0=ALU.mult, op1=ALU.add))
                p.vop(lambda E, qc=qc: E.tensor_tensor(
                    out=st1[:, 48 + qc:49 + qc], in0=sca[:, I_T2:I_T2 + 1],
                    in1=sca[:, I_T1:I_T1 + 1], op=ALU.mult))

        p.barrier()

        with ExitStack() as S5:
            h2_fm = sb("h2_fm", [128, 8 * 1024], st=S5)
            y_acc = sb("y_acc", [128, 8 * 1024], st=S5)
            sg_sb = sb("sg_sb", [128, 8 * 512], st=S5)
            gt_sb = sb("gt_sb", [128, 11 * 1024], st=S5)
            g2t_sb = gt_sb
            gbuf = sb("gbuf", [128, 2 * 512], st=S5)
            wmv = sb("wmv", [128, 2 * 512], st=S5)
            wdmv = sb("wdmv", [128, 2 * 1024], st=S5)
            xb5 = sb("p5_xb", [128, 1024], st=S5)
            hb5 = sb("p5_hb", [128, 1024], st=S5)
            pG = [psm(f"p5_pg{i}", [128, 512], st=S5) for i in range(8)]

            row2rep(mrow(4), rep_a, pG[0])
            row2rep(mod_row[0:1, 3 * C_:4 * C_], rep_b, pG[0])
            row2rep(mrow(5), rep_c, pG[0])

            FCH = [(0, 512), (512, 512), (1024, 384)]
            FCH2 = [(i * 512, 512) for i in range(5)] + [(2560, 256)]
            vev = [None]

            xfree = [None]
            for rnd in range(2):
                for tl in range(8):
                    qc = rnd * 8 + tl
                    xb = xb5[:, 0:1024]
                    if rnd + tl > 0:
                        p.wait("sync", "vec", xfree[0])
                    p.dma("sync", xb, x1_s[qc * 128:(qc + 1) * 128, :],
                          inc="lx5")
                    p.wait("vector", "lx5")
                    p.op("vector", lambda E, xb=xb, qc=qc: E.tensor_scalar(
                        out=hb5[:, 0:1024], in0=xb,
                        scalar1=st1[:, 32 + qc:33 + qc],
                        scalar2=st1[:, 48 + qc:49 + qc], op0=ALU.add,
                        op1=ALU.mult), inc="vec")
                    xfree[0] = p.mark("vec")
                    p.op("vector", lambda E: E.tensor_tensor(
                        out=hb5[:, 0:1024], in0=hb5[:, 0:1024], in1=rep_a[:],
                        op=ALU.mult))
                    p.op("vector", lambda E: E.tensor_tensor(
                        out=hb5[:, 0:1024], in0=hb5[:, 0:1024], in1=rep_b[:],
                        op=ALU.add), inc="vec")
                    p.wait("tensor", "vec")
                    for k in range(8):
                        p.op("tensor", lambda E, k=k: E.transpose(
                            pG[k % 4][:, 0:128],
                            hb5[:, k * 128:(k + 1) * 128], ident[:]),
                            inc="ten" if k % 4 == 3 else None)
                        if k % 4 == 3:
                            p.wait("vector", "ten")
                            for kk in range(k - 3, k + 1):
                                p.op("vector",
                                     lambda E, kk=kk, tl=tl: E.tensor_copy(
                                         out=h2_fm[:, kk * 1024 + tl * 128:
                                                   kk * 1024 + (tl + 1) * 128],
                                         in_=pG[kk % 4][:, 0:128]),
                                     inc="vec" if kk == k else None)
                            p.wait("tensor", "vec")
                    for ic in range(8):
                        p.op("tensor", lambda E, ic=ic, tl=tl: E.matmul(
                            pG[4][:, 0:8],
                            h2_fm[:, ic * 1024 + tl * 128:
                                  ic * 1024 + (tl + 1) * 128],
                            gwfm[:, ic * 8:(ic + 1) * 8],
                            start=(ic == 0), stop=(ic == 7)),
                            inc="ten" if ic == 7 else None)
                    p.wait("vector", "ten")
                    lg = pG[4][:, 0:8]
                    p.vop(lambda E, lg=lg: E.tensor_reduce(
                        out=sca[:, I_T0:I_T0 + 1], in_=lg, axis=AX.X,
                        op=ALU.max, negate=True))
                    p.wait("scalar", "vec")
                    p.op("scalar", lambda E, lg=lg: E.activation(
                        out=hb5[:, 0:8], in_=lg, func=AF.Exp,
                        bias=sca[:, I_T0:I_T0 + 1], scale=1.0), inc="sca")
                    p.wait("vector", "sca")
                    p.vop(lambda E: E.tensor_scalar(
                        out=hb5[:, 8:16], in0=hb5[:, 0:8], scalar1=1.0,
                        scalar2=None, op0=ALU.mult, op1=ALU.add,
                        accum_out=sca[:, I_T0:I_T0 + 1]))
                    p.vop(lambda E: E.reciprocal(
                        out=sca[:, I_T1:I_T1 + 1], in_=sca[:, I_T0:I_T0 + 1]))
                    p.vop(lambda E: E.tensor_scalar(
                        out=hb5[:, 0:8], in0=hb5[:, 0:8],
                        scalar1=sca[:, I_T1:I_T1 + 1], scalar2=None,
                        op0=ALU.mult))
                    p.vop(lambda E: E.memset(hb5[:, 8:16], 0.0))
                    for j in range(8):
                        p.vop(lambda E, j=j: E.scalar_tensor_tensor(
                            out=hb5[:, 8:16], in0=hb5[:, 0:8],
                            scalar=hb5[:, j:j + 1], in1=hb5[:, 8:16],
                            op0=ALU.is_lt, op1=ALU.add))
                    for j in range(7):
                        p.vop(lambda E, j=j: E.scalar_tensor_tensor(
                            out=hb5[:, 9 + j:16], in0=hb5[:, 1 + j:8],
                            scalar=hb5[:, j:j + 1], in1=hb5[:, 9 + j:16],
                            op0=ALU.is_equal, op1=ALU.add))
                    p.vop(lambda E: E.tensor_scalar(
                        out=hb5[:, 8:16], in0=hb5[:, 8:16], scalar1=1.5,
                        scalar2=None, op0=ALU.is_lt))
                    p.vop(lambda E: E.tensor_tensor(
                        out=hb5[:, 0:8], in0=hb5[:, 0:8], in1=hb5[:, 8:16],
                        op=ALU.mult))
                    p.vop(lambda E: E.tensor_scalar(
                        out=hb5[:, 8:16], in0=hb5[:, 0:8], scalar1=1.0,
                        scalar2=None, op0=ALU.mult, op1=ALU.add,
                        accum_out=sca[:, I_T0:I_T0 + 1]))
                    p.vop(lambda E: E.reciprocal(
                        out=sca[:, I_T1:I_T1 + 1], in_=sca[:, I_T0:I_T0 + 1]))
                    p.vop(lambda E: E.tensor_scalar(
                        out=sca[:, I_T1:I_T1 + 1], in0=sca[:, I_T1:I_T1 + 1],
                        scalar1=2.5, scalar2=None, op0=ALU.mult))
                    p.vop(lambda E, tl=tl: E.tensor_scalar(
                        out=comb[:, tl * 8:(tl + 1) * 8], in0=hb5[:, 0:8],
                        scalar1=sca[:, I_T1:I_T1 + 1], scalar2=None,
                        op0=ALU.mult))
                    p.wait("tensor", "vec")

                p.op("vector", lambda E: E.memset(y_acc[:], 0.0), inc="vec")

                for e in range(E_):
                    for fci, (fo, fsz) in enumerate(FCH):
                        for gu, wsrc in ((0, w_gate), (1, w_up)):
                            for ic in range(8):
                                buf = wmv[:, (ic % 2) * 512:(ic % 2) * 512 + fsz]
                                deng = "sync" if ic % 2 == 0 else "gpsimd"
                                p.wait(deng, "ten",
                                       p.mark("ten") - 1)
                                lms = "lm0" if ic % 2 == 0 else "lm1"
                                p.dma(deng, buf,
                                      dslice(wsrc, e * C_ * F_ + ic * 128 * F_
                                             + fo, [[F_, 128], [1, fsz]]),
                                      inc=lms)
                                p.wait("tensor", lms)
                                for tl in range(8):
                                    if ic == 0 and gu == 1:
                                        p.wait("tensor", "sca", s0m + tl + 1)
                                    if (ic == 0 and gu == 0
                                            and vev[0] is not None):
                                        p.wait("tensor", "vec",
                                               vev[0] + tl + 1)
                                    p.op("tensor",
                                         lambda E, ic=ic, tl=tl, buf=buf,
                                         fsz=fsz: E.matmul(
                                             pG[tl][:, 0:fsz],
                                             h2_fm[:, ic * 1024 + tl * 128:
                                                   ic * 1024 + (tl + 1) * 128],
                                             buf, start=(ic == 0),
                                             stop=(ic == 7)),
                                         inc="ten" if tl == 7 else None)
                            if gu == 0:
                                s0m = p.mark("sca")
                                p.wait("scalar", "ten")
                                for tl in range(8):
                                    p.op("scalar",
                                         lambda E, tl=tl, fsz=fsz:
                                         E.activation(
                                             out=sg_sb[:, tl * 512:
                                                       tl * 512 + fsz],
                                             in_=pG[tl][:, 0:fsz],
                                             func=AF.Silu),
                                         inc="sca")
                        p.wait("vector", "ten")
                        nsub = (fsz + 127) // 128
                        fc0 = fo // 128
                        t0g = p.mark("ten")
                        v0g = p.mark("vec")

                        def stt_g(tl, fsz=fsz, e=e):
                            gb_ = gbuf[:, (tl % 2) * 512:(tl % 2) * 512 + fsz]
                            if tl >= 2:
                                p.wait("vector", "ten", t0g + tl - 1)
                            p.op("vector", lambda E, tl=tl, gb_=gb_, e=e,
                                 fsz=fsz: E.scalar_tensor_tensor(
                                     out=gb_, in0=pG[tl][:, 0:fsz],
                                     scalar=comb[:, tl * 8 + e:tl * 8 + e + 1],
                                     in1=sg_sb[:, tl * 512:tl * 512 + fsz],
                                     op0=ALU.mult, op1=ALU.mult), inc="vec")

                        def copy_g(tl, fc0=fc0, nsub=nsub):
                            p.wait("vector", "ten", t0g + tl + 1)
                            p.op("vector", lambda E, tl=tl, fc0=fc0,
                                 nsub=nsub: E.tensor_copy(
                                     out=dslice(gt_sb.ap(),
                                                fc0 * 1024 + tl * 128,
                                                [[11 * 1024, 128],
                                                 [1024, nsub], [1, 128]]),
                                     in_=dslice(pG[tl].ap(), 0,
                                                [[512, 128], [128, nsub],
                                                 [1, 128]])), inc="vec")

                        stt_g(0)
                        stt_g(1)
                        for tl in range(2, 8):
                            copy_g(tl - 2)
                            stt_g(tl)
                        copy_g(6)
                        copy_g(7)
                        for tl in range(8):
                            p.wait("tensor", "vec",
                                   v0g + (1 if tl == 0 else 2 * tl))
                            gb_ = gbuf[:, (tl % 2) * 512:(tl % 2) * 512 + fsz]
                            for k in range(nsub):
                                p.op("tensor", lambda E, tl=tl, k=k, gb_=gb_:
                                     E.transpose(
                                         pG[tl][:, k * 128:(k + 1) * 128],
                                         gb_[:, k * 128:(k + 1) * 128],
                                         ident[:]),
                                     inc="ten" if k == nsub - 1 else None)
                        p.wait("tensor", "vec")
                    for tg in range(2):
                        for fc in range(11):
                            buf = wdmv[:, (fc % 2) * 1024:(fc % 2 + 1) * 1024]
                            deng = "sync" if fc % 2 == 0 else "gpsimd"
                            p.wait(deng, "ten",
                                   p.mark("ten") - 1)
                            lds = "lda" if fc % 2 == 0 else "ldb"
                            p.dma(deng, buf,
                                  dslice(w_down, e * F_ * C_ + fc * 128 * C_,
                                         [[C_, 128], [1, 1024]]), inc=lds)
                            p.wait("tensor", lds)
                            if fc == 0 and tg == 0:
                                p.wait("tensor", "vec", p.mark("vec"))
                            for ti in range(4):
                                tl = tg * 4 + ti
                                for half in range(2):
                                    if fc == 0 and tg == 1:
                                        p.wait("tensor", "vec",
                                               vev[0] + 2 * ti + half + 1)
                                    p.op("tensor",
                                         lambda E, fc=fc, tl=tl, ti=ti,
                                         half=half, buf=buf: E.matmul(
                                             pG[2 * ti + half][:, 0:512],
                                             gt_sb[:, fc * 1024 + tl * 128:
                                                   fc * 1024 + (tl + 1) * 128],
                                             buf[:, half * 512:
                                                 (half + 1) * 512],
                                             start=(fc == 0), stop=(fc == 10)),
                                         inc="ten" if (ti == 3
                                                       and half == 1)
                                         else None)
                        p.wait("vector", "ten")
                        vev[0] = p.mark("vec")
                        for ti in range(4):
                            tl = tg * 4 + ti
                            for half in range(2):
                                p.op("vector",
                                     lambda E, ti=ti, tl=tl, half=half:
                                     E.scalar_tensor_tensor(
                                         out=y_acc[:, tl * 1024 + half * 512:
                                                   tl * 1024 +
                                                   (half + 1) * 512],
                                         in0=pG[2 * ti + half][:, 0:512],
                                         scalar=1.0,
                                         in1=y_acc[:, tl * 1024 + half * 512:
                                                   tl * 1024 +
                                                   (half + 1) * 512],
                                         op0=ALU.mult, op1=ALU.add),
                                     inc="vec")

                for tg in range(2):
                    for fci, (fo, fsz) in enumerate(FCH2):
                        for gu, wsrc in ((0, sw_gate), (1, sw_up)):
                            for ic in range(8):
                                buf = wmv[:, (ic % 2) * 512:(ic % 2) * 512 + fsz]
                                deng = "sync" if ic % 2 == 0 else "gpsimd"
                                p.wait(deng, "ten",
                                       p.mark("ten") - 1)
                                lms = "lm0" if ic % 2 == 0 else "lm1"
                                p.dma(deng, buf,
                                      dslice(wsrc, ic * 128 * FS_ + fo,
                                             [[FS_, 128], [1, fsz]]),
                                      inc=lms)
                                p.wait("tensor", lms)
                                for ti in range(4):
                                    tl = tg * 4 + ti
                                    if ic == 0 and gu == 1:
                                        p.wait("tensor", "sca", s0s + ti + 1)
                                    if (ic == 0 and gu == 0
                                            and vev[0] is not None):
                                        p.wait("tensor", "vec",
                                               vev[0] + ti + 1)
                                    p.op("tensor",
                                         lambda E, ic=ic, ti=ti, tl=tl,
                                         buf=buf, fsz=fsz: E.matmul(
                                             pG[ti][:, 0:fsz],
                                             h2_fm[:, ic * 1024 + tl * 128:
                                                   ic * 1024 + (tl + 1) * 128],
                                             buf, start=(ic == 0),
                                             stop=(ic == 7)),
                                         inc="ten" if ti == 3 else None)
                            if gu == 0:
                                s0s = p.mark("sca")
                                p.wait("scalar", "ten")
                                for ti in range(4):
                                    p.op("scalar",
                                         lambda E, ti=ti, fsz=fsz:
                                         E.activation(
                                             out=sg_sb[:, ti * 512:
                                                       ti * 512 + fsz],
                                             in_=pG[ti][:, 0:fsz],
                                             func=AF.Silu),
                                         inc="sca")
                        p.wait("vector", "ten")
                        nsub = (fsz + 127) // 128
                        fc0 = fo // 128
                        t0g = p.mark("ten")
                        v0g = p.mark("vec")

                        def stt_s(ti, fsz=fsz):
                            gb_ = gbuf[:, (ti % 2) * 512:(ti % 2) * 512 + fsz]
                            if ti >= 2:
                                p.wait("vector", "ten", t0g + ti - 1)
                            p.op("vector", lambda E, ti=ti, gb_=gb_, fsz=fsz:
                                 E.tensor_tensor(
                                     out=gb_, in0=pG[ti][:, 0:fsz],
                                     in1=sg_sb[:, ti * 512:ti * 512 + fsz],
                                     op=ALU.mult), inc="vec")

                        def copy_s(ti, fc0=fc0, nsub=nsub):
                            p.wait("vector", "ten", t0g + ti + 1)
                            p.op("vector", lambda E, fc0=fc0, ti=ti,
                                 nsub=nsub: E.tensor_copy(
                                     out=dslice(g2t_sb.ap(),
                                                fc0 * 512 + ti * 128,
                                                [[22 * 512, 128],
                                                 [512, nsub], [1, 128]]),
                                     in_=dslice(pG[ti].ap(), 0,
                                                [[512, 128], [128, nsub],
                                                 [1, 128]])), inc="vec")

                        stt_s(0)
                        stt_s(1)
                        for ti in range(2, 4):
                            copy_s(ti - 2)
                            stt_s(ti)
                        copy_s(2)
                        copy_s(3)
                        for ti in range(4):
                            p.wait("tensor", "vec",
                                   v0g + (1 if ti == 0 else 2 * ti))
                            gb_ = gbuf[:, (ti % 2) * 512:(ti % 2) * 512 + fsz]
                            for k in range(nsub):
                                p.op("tensor", lambda E, ti=ti, k=k, gb_=gb_:
                                     E.transpose(
                                         pG[ti][:, k * 128:(k + 1) * 128],
                                         gb_[:, k * 128:(k + 1) * 128],
                                         ident[:]),
                                     inc="ten" if k == nsub - 1 else None)
                        p.wait("tensor", "vec")
                    for fc in range(22):
                        buf = wdmv[:, (fc % 2) * 1024:(fc % 2 + 1) * 1024]
                        deng = "sync" if fc % 2 == 0 else "gpsimd"
                        p.wait(deng, "ten",
                               p.mark("ten") - 1)
                        lds = "lda" if fc % 2 == 0 else "ldb"
                        p.dma(deng, buf,
                              dslice(sw_down, fc * 128 * C_,
                                     [[C_, 128], [1, 1024]]), inc=lds)
                        p.wait("tensor", lds)
                        if fc == 0 and tg == 0:
                            p.wait("tensor", "vec", p.mark("vec"))
                        for ti in range(4):
                            for half in range(2):
                                if fc == 0 and tg == 1:
                                    p.wait("tensor", "vec",
                                           vev[0] + 2 * ti + half + 1)
                                p.op("tensor",
                                     lambda E, fc=fc, ti=ti, half=half,
                                     buf=buf: E.matmul(
                                         pG[2 * ti + half][:, 0:512],
                                         g2t_sb[:, fc * 512 + ti * 128:
                                                fc * 512 + (ti + 1) * 128],
                                         buf[:, half * 512:(half + 1) * 512],
                                         start=(fc == 0), stop=(fc == 21)),
                                     inc="ten" if (ti == 3
                                                   and half == 1) else None)
                    p.wait("vector", "ten")
                    vev[0] = p.mark("vec")
                    for ti in range(4):
                        tl = tg * 4 + ti
                        for half in range(2):
                            p.op("vector", lambda E, ti=ti, tl=tl, half=half:
                                 E.scalar_tensor_tensor(
                                     out=y_acc[:, tl * 1024 + half * 512:
                                               tl * 1024 + (half + 1) * 512],
                                     in0=pG[2 * ti + half][:, 0:512],
                                     scalar=1.0,
                                     in1=y_acc[:, tl * 1024 + half * 512:
                                               tl * 1024 + (half + 1) * 512],
                                     op0=ALU.mult, op1=ALU.add), inc="vec")

                for tl in range(8):
                    qc = rnd * 8 + tl
                    xb = xb5[:, 0:1024]
                    p.wait("sync", "vec", p.mark("vec"))
                    p.dma("sync", xb, x1_s[qc * 128:(qc + 1) * 128, :],
                          inc="lx5")
                    p.wait("vector", "lx5")
                    p.wait("vector", "st", p.mark("st"))
                    p.op("vector", lambda E, tl=tl: E.tensor_tensor(
                        out=hb5[:, 0:1024], in0=y_acc[:, tl * 1024:
                                                     (tl + 1) * 1024],
                        in1=rep_c[:], op=ALU.mult))
                    p.op("vector", lambda E, xb=xb: E.tensor_tensor(
                        out=hb5[:, 0:1024], in0=hb5[:, 0:1024], in1=xb,
                        op=ALU.add), inc="vec")
                    xfree[0] = p.mark("vec")
                    p.wait("gpsimd", "vec")
                    p.dma("gpsimd", out_hbm[qc * 128:(qc + 1) * 128, :],
                          hb5[:, 0:1024])

        p.wait("gpsimd", "st")

        with nc.Block() as block:
            p.finalize(block)
    return nc


_NC_CACHE = []


def kernel(**inputs):
    x = np.asarray(inputs["L0_attn_x"], np.float32)
    wqkv = np.ascontiguousarray(inputs["L0_attn_Wqkv"], np.float32)
    bqkv = np.asarray(inputs["L0_attn_bqkv"], np.float32)
    wproj = np.ascontiguousarray(inputs["L0_attn_Wproj"], np.float32)
    bproj = np.asarray(inputs["L0_attn_bproj"], np.float32)
    gate_w = np.ascontiguousarray(inputs["L0_ffn_gate_w"], np.float32)
    w_gate = np.ascontiguousarray(inputs["L0_ffn_w_gate"], np.float32)
    w_up = np.ascontiguousarray(inputs["L0_ffn_w_up"], np.float32)
    w_down = np.ascontiguousarray(inputs["L0_ffn_w_down"], np.float32)
    sw_gate = np.ascontiguousarray(inputs["L0_ffn_sw_gate"], np.float32)
    sw_up = np.ascontiguousarray(inputs["L0_ffn_sw_up"], np.float32)
    sw_down = np.ascontiguousarray(inputs["L0_ffn_sw_down"], np.float32)
    c_ = np.ascontiguousarray(inputs["L0_c"], np.float32)
    wmod = np.ascontiguousarray(inputs["L0_wmod"], np.float32)

    ident = np.eye(128, dtype=np.float32)
    kk = np.arange(128)[:, None]
    qq = np.arange(256)[None, :]
    tri = np.concatenate(
        [np.where(kk <= qq, 0.0, NEG).astype(np.float32),
         np.where(kk + 128 <= qq, 0.0, NEG).astype(np.float32)], axis=1)
    bqkv_fmv = np.ascontiguousarray(bqkv.reshape(24, 128).T)
    bqkv_vrow = np.ascontiguousarray(bqkv[2 * C_:].reshape(1, C_))

    in_maps = []
    perms = []
    for core in range(8):
        b, par = core // 2, core % 2
        ob = own_blocks(par)
        tb = own_blocks(1 - par)
        perm = np.concatenate([np.arange(256 * blk, 256 * (blk + 1))
                               for blk in ob])
        perm_o = np.concatenate([np.arange(256 * blk, 256 * (blk + 1))
                                 for blk in tb])
        perms.append(perm)
        for i in range(QT):
            for j in range(i):
                assert tb[j] < ob[i]
        kill = np.zeros((128, QT), np.float32)
        for i in range(QT):
            if not (tb[i] < ob[i]):
                kill[:, i] = NEG
        m = dict(
            x_own=np.ascontiguousarray(x[b][perm]),
            x_oth=np.ascontiguousarray(x[b][perm_o]),
            wqkv=wqkv, bqkv_fmv=bqkv_fmv, bqkv_vrow=bqkv_vrow,
            wproj=wproj, bproj_r=bproj.reshape(1, C_),
            gate_w_in=gate_w, w_gate=w_gate, w_up=w_up, w_down=w_down,
            sw_gate=sw_gate, sw_up=sw_up, sw_down=sw_down,
            c_row_in=c_[b:b + 1], wmod=wmod,
            ident_in=ident, tri_in=tri, kill_in=kill,
        )
        in_maps.append(m)

    if not _NC_CACHE:
        _NC_CACHE.append(build_program())
    nc = _NC_CACHE[0]
    res = run_bass_kernel_spmd(nc, in_maps, core_ids=list(range(8)))
    out = np.empty((4, 4096, C_), np.float32)
    for core in range(8):
        b = core // 2
        out[b][perms[core]] = res.results[core]["out"]
    return out
